# version 03 (final)
# baseline (speedup 1.0000x reference)
import numpy as np
import ml_dtypes
import concourse.bass as bass
import concourse.mybir as mybir
from concourse.bass_utils import run_bass_kernel_spmd

F32 = mybir.dt.float32
BF16 = mybir.dt.bfloat16
I32 = mybir.dt.int32
AF = mybir.ActivationFunctionType
ALU = mybir.AluOpType
AX = mybir.AxisListType


class Cfg:
    def __init__(self, B, S, D, I, NH, NKV, HD=128, TT=None):
        self.B, self.S, self.D, self.I = B, S, D, I
        self.NH, self.NKV, self.HD = NH, NKV, HD
        self.KV = NKV * HD
        self.T = B * S // 8
        self.TPS = S // 128
        self.NT = self.TPS // 2
        assert self.T == self.NT * 128 * (B // 4) or True
        assert self.T == self.NT * 128, (self.T, self.NT)
        self.TT = TT if TT is not None else min(512, self.T)
        self.NTT = self.T // self.TT
        self.DC = D // 128
        self.IC = I // 128
        self.ICH = (self.IC + 1) // 2
        self.OCQ = NH * HD // 128
        self.OCK = self.KV // 128
        self.HC = NH * HD // 128
        self.SCALE = HD ** -0.5
        self.EPS = 1e-5
        self.SPAN = [self.TPS - 2 * k for k in range(self.NT)]
        self.PH = (self.T + 511) // 512
        self.PHW = self.T // self.PH
        self.NREP = NH // NKV


CFG_FULL = Cfg(B=4, S=2048, D=4096, I=11008, NH=32, NKV=8)
CFG_SMALL = Cfg(B=4, S=512, D=512, I=768, NH=4, NKV=2)
CFG_MED = Cfg(B=4, S=2048, D=2048, I=1024, NH=16, NKV=8)
CFG_TINY2 = Cfg(B=4, S=512, D=512, I=768, NH=4, NKV=2, TT=128)


class Sem:
    def __init__(self, handle, producer_engine=None, is_dma=False, bulk=False):
        self.h = handle
        self.n = 0
        self.producer = producer_engine
        self.is_dma = is_dma
        self.bulk = bulk

    def tick(self, k):
        self.n += k
        return self.n


class Builder:
    ENGS = ("tensor", "vector", "scalar", "gpsimd", "sync")

    def __init__(self, nc):
        self.nc = nc
        self.plan = {e: [] for e in self.ENGS}
        self.hw = {e: {} for e in self.ENGS}
        self.w = {}
        self.r = {}

    def emit(self, eng, fn, reads=(), writes=(), inc=None, inc_n=1):
        waits = {}

        def add_wait(sem, tok):
            if sem.bulk:
                tok = sem.n
            if self.hw[eng].get(sem, 0) >= tok:
                return
            waits[sem] = max(waits.get(sem, 0), tok)

        for key in reads:
            for sem, tok in self.w.get(key, {}).items():
                add_wait(sem, tok)
        for key in writes:
            for sem, tok in self.w.get(key, {}).items():
                add_wait(sem, tok)
            for sem, tok in self.r.get(key, {}).items():
                add_wait(sem, tok)
        assert inc is not None
        tok_after = inc.tick(inc_n if inc_n is not None else 1)
        for sem, tokv in waits.items():
            self.hw[eng][sem] = max(self.hw[eng].get(sem, 0), tokv)
        wl = list(waits.items())

        def op(e, wl=wl, fn=fn, inc=inc, inc_n=inc_n):
            for sem, tokv in wl:
                e.wait_ge(sem.h, tokv)
            ins = fn(e)
            if inc_n is None:
                ins.then_inc(inc.h)
            else:
                ins.then_inc(inc.h, inc_n)
            return ins

        self.plan[eng].append(op)
        for key in reads:
            self.r.setdefault(key, {})[inc] = tok_after
        for key in writes:
            self.w[key] = {inc: tok_after}
            self.r[key] = {}
        return tok_after

    def raw_wait(self, eng, sem, tok):
        if tok <= 0:
            return
        if self.hw[eng].get(sem, 0) >= tok:
            return
        self.hw[eng][sem] = tok
        self.plan[eng].append(lambda e, s=sem, t=tok: e.wait_ge(s.h, t))

    def barrier(self, sems):
        for eng in self.ENGS:
            for s in sems:
                self.raw_wait(eng, s, s.n)
        self.w = {}
        self.r = {}


def build_nc(cfg: Cfg, upto="full"):
    c = cfg
    nc = bass.Bass("TRN2", target_bir_lowering=False)

    x_in = nc.dram_tensor("x", [c.D, c.T], F32, kind="ExternalInput").ap()
    gwT = nc.dram_tensor("gwT", [c.IC, 128, c.DC * 128], I32, kind="ExternalInput").ap()
    uwT = nc.dram_tensor("uwT", [c.IC, 128, c.DC * 128], I32, kind="ExternalInput").ap()
    dwT = nc.dram_tensor("dwT", [c.DC, 128, c.IC * 128], I32, kind="ExternalInput").ap()
    wqT = nc.dram_tensor("wqT", [c.OCQ, 128, c.DC * 128], F32, kind="ExternalInput").ap()
    wkT = nc.dram_tensor("wkT", [c.OCK, 128, c.DC * 128], F32, kind="ExternalInput").ap()
    wvT = nc.dram_tensor("wvT", [c.OCK, 128, c.DC * 128], F32, kind="ExternalInput").ap()
    woT = nc.dram_tensor("woT", [c.DC, 128, c.HC * 128], F32, kind="ExternalInput").ap()
    scal_in = nc.dram_tensor("scal", [1, 3], F32, kind="ExternalInput").ap()
    masks_in = nc.dram_tensor("masks", [128, c.NT * 2 * 128], BF16, kind="ExternalInput").ap()
    ident_in = nc.dram_tensor("ident", [128, 128], BF16, kind="ExternalInput").ap()
    out_d = nc.dram_tensor("out", [c.D, c.T], F32, kind="ExternalOutput").ap()

    xmid = nc.dram_tensor("xmid", [c.D, c.T], F32).ap()
    xattn = nc.dram_tensor("xattn", [c.D, c.T], F32).ap()
    if upto != "ffn1":
        assert c.T == c.KV, "kv exchange layout assumes T == NKV*HD"
    kv_contrib_t = nc.dram_tensor("kv_contrib", [c.KV + c.T, c.T], BF16)
    kv_gath_t = nc.dram_tensor("kv_gath", [2 * (c.KV + c.T), c.T], BF16)
    kv_contrib = kv_contrib_t.ap()
    kv_gath = kv_gath_t.ap()

    sems = []
    sem_cms = []

    def mksem(name, producer=None, is_dma=False, bulk=False):
        cm = nc.semaphore(name)
        s = Sem(cm.__enter__(), producer, is_dma, bulk)
        sem_cms.append(cm)
        sems.append(s)
        return s

    s_mm = mksem("s_mm", "tensor")
    s_ve = mksem("s_ve", "vector")
    s_se = mksem("s_se", "scalar")
    s_const = mksem("s_const", "sync", True, bulk=True)
    s_cc = mksem("s_cc", "gpsimd", True)
    s_kvl = mksem("s_kvl", "sync", True, bulk=True)
    s_stkv = mksem("s_stkv", "sync", True, bulk=True)
    sl_wg = [mksem(f"s_wg{i}", "gpsimd", True) for i in range(2)]
    sl_wu = [mksem(f"s_wu{i}", "gpsimd", True) for i in range(2)]
    sl_wd = [mksem(f"s_wd{i}", "gpsimd", True) for i in range(2)]
    sl_wa = [mksem(f"s_wa{i}", "gpsimd", True) for i in range(2)]
    sl_xbf = [mksem(f"s_xbf{i}", "gpsimd", True) for i in range(2)]
    sl_xf = [mksem(f"s_xf{i}", "sync", True) for i in range(2)]
    sl_stm = [mksem(f"s_stm{i}", "sync", True) for i in range(2)]
    sl_sta = [mksem(f"s_sta{i}", "sync", True) for i in range(2)]
    sl_sto = [mksem(f"s_sto{i}", "sync", True) for i in range(2)]

    bld = Builder(nc)

    base = (nc.sbuf_base + 31) // 32 * 32
    top = nc.sbuf_top
    off = [base]

    def alloc(name, shape, dtype, at=None):
        nbytes = int(np.prod(shape[1:])) * mybir.dt.size(dtype)
        if at is None:
            o = (off[0] + 31) // 32 * 32
            off[0] = o + nbytes
            assert off[0] <= top, f"SBUF overflow at {name}: {off[0] - base} used"
        else:
            o = at
            assert o % 32 == 0 and o + nbytes <= top
        return nc.alloc_sbuf_tensor_at(name, shape, dtype, offset=o), o

    ones_bf, _ = alloc("ones_bf", [128, 16], BF16)
    ones_row_f, _ = alloc("ones_row_f", [1, 128], F32)
    scal, _ = alloc("scal_sb", [1, 8], F32)
    gs_bc, _ = alloc("gs_bc", [128, 8], F32)
    cout_bc, _ = alloc("cout_bc", [128, 8], F32)
    rows, _ = alloc("rows", [1, 4 * c.TT], F32)
    sums_row, _ = alloc("sums_row", [1, 256], F32)
    rec_sb, _ = alloc("rec_sb", [128, 256], F32)
    xbf, _ = alloc("xbf", [128, 4 * c.TT], BF16)
    ptb, _ = alloc("ptb", [128, 4 * 512], BF16)
    sqb = ptb
    sgb, _ = alloc("sgb", [128, 2 * c.TT], F32)
    tmpn, _ = alloc("tmpn", [128, c.TT], F32)
    xf32, _ = alloc("xf32", [128, 2 * c.TT], F32)
    xmo, _ = alloc("xmo", [128, 2 * c.TT], F32)

    phase_base = (off[0] + 31) // 32 * 32

    offA = [phase_base]

    def allocA(name, shape, dtype):
        nbytes = int(np.prod(shape[1:])) * mybir.dt.size(dtype)
        o = (offA[0] + 31) // 32 * 32
        offA[0] = o + nbytes
        assert offA[0] <= top, f"SBUF overflow (A) at {name}: {offA[0] - base}"
        return nc.alloc_sbuf_tensor_at(name, shape, dtype, offset=o)

    xln = allocA("xln", [128, c.DC * c.TT], BF16)
    hh = allocA("hh", [128, c.IC * c.TT], BF16)
    wgs = allocA("wgs", [128, 2 * c.DC * 128], BF16)
    wus = allocA("wus", [128, 2 * c.DC * 128], BF16)
    wds = allocA("wds", [128, 2 * c.ICH * 128], BF16)

    offB = [phase_base]

    def allocB(name, shape, dtype):
        nbytes = int(np.prod(shape[1:])) * mybir.dt.size(dtype)
        o = (offB[0] + 31) // 32 * 32
        offB[0] = o + nbytes
        assert offB[0] <= top, f"SBUF overflow (B) at {name}: {offB[0] - base}"
        return nc.alloc_sbuf_tensor_at(name, shape, dtype, offset=o), o

    q_sb, _ = allocB("q_sb", [128, c.NH * c.T], BF16)
    VAW = c.NKV * 129
    kv_bytes = c.NKV * c.S * 2 + (c.S // 128) * VAW * 2
    xln2, o_xln2 = allocB("xln2", [128, max(c.DC * c.T, (kv_bytes + 1) // 2)], BF16)
    wa, _ = allocB("wa", [128, 2 * c.DC * 128], BF16)
    kst, _ = allocB("kst", [128, c.T], BF16)
    vst, _ = allocB("vst", [128, c.T], BF16)
    vtmst, _ = allocB("vtmst", [128, c.T], BF16)
    masks, _ = allocB("masks", [128, c.NT * 2 * 128], BF16)
    tmp_tm, _ = allocB("tmp_tm", [128, 2 * 128], BF16)
    ident, _ = allocB("ident", [128, 128], BF16)
    k_full = nc.alloc_sbuf_tensor_at("k_full", [128, c.NKV * c.S], BF16, offset=o_xln2)
    v_aug = nc.alloc_sbuf_tensor_at("v_aug", [128, (c.S // 128) * VAW], BF16,
                                    offset=o_xln2 + c.NKV * c.S * 2)

    psum_cms = []

    def mkpsum(name, shape, dtype):
        cm = nc.psum_tensor(name, shape, dtype)
        t = cm.__enter__()
        psum_cms.append(cm)
        return t

    PB = [mkpsum(f"PB{i}", [128, 512], F32) for i in range(7)]
    PBT = mkpsum("PBT", [128, 1024], BF16)

    bld.emit("sync", lambda e: e.dma_start(scal[0:1, 0:3], scal_in), writes=["scal"], inc=s_const, inc_n=16)
    bld.emit("vector", lambda e: e.memset(ones_bf[:], 1.0), writes=["ones_bf"], inc=s_ve)
    bld.emit("vector", lambda e: e.memset(ones_row_f[:], 1.0), writes=["ones_row_f"], inc=s_ve)
    bld.emit("vector", lambda e: e.tensor_tensor(scal[0:1, 3:4], scal[0:1, 1:2], scal[0:1, 2:3], ALU.mult),
             reads=["scal"], writes=["scal"], inc=s_ve)
    bld.emit("vector", lambda e: e.tensor_scalar(scal[0:1, 3:4], scal[0:1, 3:4], 0.5, None, ALU.mult),
             reads=["scal"], writes=["scal"], inc=s_ve)
    bld.emit("tensor", lambda e: e.matmul(PB[0][:, 0:1], ones_row_f[:], scal[0:1, 0:1], start=True, stop=True),
             reads=["ones_row_f", "scal"], writes=["PB0"], inc=s_mm)
    bld.emit("tensor", lambda e: e.matmul(PB[1][:, 0:1], ones_row_f[:], scal[0:1, 3:4], start=True, stop=True),
             reads=["ones_row_f", "scal"], writes=["PB1"], inc=s_mm)
    bld.emit("vector", lambda e: e.tensor_copy(gs_bc[:, 0:1], PB[0][:, 0:1]), reads=["PB0"], writes=["gs_bc"], inc=s_ve)
    bld.emit("vector", lambda e: e.tensor_copy(cout_bc[:, 0:1], PB[1][:, 0:1]), reads=["PB1"], writes=["cout_bc"], inc=s_ve)

    XG = 2

    def emit_ln(src, src_key, col0, xln_t, dst_key, proj):
        TTw = c.TT
        for phase in range(2):
            for g in range(c.DC // XG):
                slot = g % 2
                xsl = xbf[:, slot * XG * c.TT: slot * XG * c.TT + XG * TTw]

                def ldx(e, g=g, xsl=xsl, col0=col0):
                    src_ap = src.rearrange("(dc p) t -> p dc t", p=128)[:, g * XG:(g + 1) * XG, col0:col0 + TTw]
                    return e.dma_start(xsl.rearrange("p (dc t) -> p dc t", dc=XG), src_ap)

                bld.emit("gpsimd", ldx, reads=[src_key], writes=[f"xbf{slot}"],
                         inc=sl_xbf[slot], inc_n=16)
                for i in range(XG):
                    dc = g * XG + i
                    xc = xbf[:, (slot * XG + i) * c.TT: (slot * XG + i) * c.TT + TTw]
                    if phase == 0:
                        sq = sqb[:, (slot * XG + i) * c.TT: (slot * XG + i) * c.TT + TTw]
                        bld.emit("scalar", lambda e, sq=sq, xc=xc: e.activation(sq, xc, AF.Square),
                                 reads=[f"xbf{slot}"], writes=[f"sqb{slot}"], inc=s_se)
                        bld.emit("tensor", lambda e, xc=xc, dc=dc: e.matmul(
                            PB[3][0:1, 0:TTw], ones_bf[:, 0:1], xc, start=(dc == 0), stop=(dc == c.DC - 1)),
                            reads=[f"xbf{slot}", "ones_bf"], writes=["PB3"], inc=s_mm)
                        bld.emit("tensor", lambda e, sq=sq, dc=dc: e.matmul(
                            PB[4][0:1, 0:TTw], ones_bf[:, 0:1], sq, start=(dc == 0), stop=(dc == c.DC - 1)),
                            reads=[f"sqb{slot}", "ones_bf"], writes=["PB4"], inc=s_mm)
                    else:
                        if proj:
                            dst = xln_t[:, dc * c.T + col0: dc * c.T + col0 + TTw]
                        else:
                            dst = xln_t[:, dc * c.TT: dc * c.TT + TTw]
                        bld.emit("vector", lambda e, xc=xc: e.tensor_tensor(
                            tmpn[:, 0:TTw], xc, PB[5][:, 0:TTw], ALU.mult),
                            reads=[f"xbf{slot}", "PB5"], writes=["tmpn"], inc=s_ve)
                        bld.emit("vector", lambda e, dst=dst: e.tensor_tensor(
                            dst, tmpn[:, 0:TTw], PB[6][:, 0:TTw], ALU.subtract),
                            reads=["tmpn", "PB6"], writes=[dst_key], inc=s_ve)
            if phase == 0:
                m = rows[0:1, 0:TTw]
                var = rows[0:1, c.TT:c.TT + TTw]
                rst = rows[0:1, 2 * c.TT:2 * c.TT + TTw]
                mr = rows[0:1, 3 * c.TT:3 * c.TT + TTw]
                inv_d = 1.0 / c.D
                bld.emit("vector", lambda e: e.tensor_scalar(m, PB[3][0:1, 0:TTw], inv_d, None, ALU.mult),
                         reads=["PB3"], writes=["row_m"], inc=s_ve)
                bld.emit("vector", lambda e: e.tensor_tensor(var, m, m, ALU.mult),
                         reads=["row_m"], writes=["row_v"], inc=s_ve)
                bld.emit("vector", lambda e: e.scalar_tensor_tensor(
                    var, PB[4][0:1, 0:TTw], inv_d, var, ALU.mult, ALU.subtract),
                    reads=["PB4", "row_v"], writes=["row_v"], inc=s_ve)
                bld.emit("vector", lambda e: e.tensor_scalar(var, var, float(c.EPS), None, ALU.add),
                         reads=["row_v"], writes=["row_v"], inc=s_ve)
                bld.emit("scalar", lambda e: e.activation(rst, var, AF.Sqrt),
                         reads=["row_v"], writes=["row_r"], inc=s_se)
                bld.emit("vector", lambda e: e.reciprocal(rst, rst),
                         reads=["row_r"], writes=["row_r"], inc=s_ve)
                bld.emit("vector", lambda e: e.tensor_tensor(mr, m, rst, ALU.mult),
                         reads=["row_m", "row_r"], writes=["row_mr"], inc=s_ve)
                bld.emit("tensor", lambda e: e.matmul(PB[5][:, 0:TTw], ones_row_f[:], rst, start=True, stop=True),
                         reads=["ones_row_f", "row_r"], writes=["PB5"], inc=s_mm)
                bld.emit("tensor", lambda e: e.matmul(PB[6][:, 0:TTw], ones_row_f[:], mr, start=True, stop=True),
                         reads=["ones_row_f", "row_mr"], writes=["PB6"], inc=s_mm)

    def emit_ffn(src, src_key, dst, dst_key, st_sems):
        for tt in range(c.NTT):
            col0 = tt * c.TT
            emit_ln(src, src_key, col0, xln, "xln", proj=False)
            for ic in range(c.IC):
                slot = ic % 2
                wg_sl = wgs[:, slot * c.DC * 128:(slot + 1) * c.DC * 128]
                wu_sl = wus[:, slot * c.DC * 128:(slot + 1) * c.DC * 128]
                bld.emit("gpsimd", lambda e, ic=ic, wg_sl=wg_sl: e.dma_start(wg_sl, gwT[ic]),
                         writes=[f"wg{slot}"], inc=sl_wg[slot], inc_n=16)
                bld.emit("gpsimd", lambda e, ic=ic, wu_sl=wu_sl: e.dma_start(wu_sl, uwT[ic]),
                         writes=[f"wu{slot}"], inc=sl_wu[slot], inc_n=16)
                pg, pu = PB[3 + slot], PB[5 + slot]

                def gmm(e, wg_sl=wg_sl, pg=pg):
                    for dc in range(c.DC):
                        ins = e.matmul(pg[:, 0:c.TT], wg_sl[:, dc * 128:(dc + 1) * 128],
                                       xln[:, dc * c.TT:(dc + 1) * c.TT],
                                       start=(dc == 0), stop=(dc == c.DC - 1))
                    return ins

                bld.emit("tensor", gmm, reads=[f"wg{slot}", "xln"], writes=[f"PB{3 + slot}"], inc=s_mm)

                def umm(e, wu_sl=wu_sl, pu=pu):
                    for dc in range(c.DC):
                        ins = e.matmul(pu[:, 0:c.TT], wu_sl[:, dc * 128:(dc + 1) * 128],
                                       xln[:, dc * c.TT:(dc + 1) * c.TT],
                                       start=(dc == 0), stop=(dc == c.DC - 1))
                    return ins

                bld.emit("tensor", umm, reads=[f"wu{slot}", "xln"], writes=[f"PB{5 + slot}"], inc=s_mm)
                sg = sgb[:, slot * c.TT:(slot + 1) * c.TT]
                bld.emit("scalar", lambda e, sg=sg, pg=pg: e.activation(
                    sg, pg[:, 0:c.TT], AF.Silu, scale=gs_bc[:, 0:1]),
                    reads=[f"PB{3 + slot}", "gs_bc"], writes=[f"sg{slot}"], inc=s_se)
                bld.emit("vector", lambda e, sg=sg, pu=pu, ic=ic: e.tensor_tensor(
                    hh[:, ic * c.TT:(ic + 1) * c.TT], sg, pu[:, 0:c.TT], ALU.mult),
                    reads=[f"sg{slot}", f"PB{5 + slot}"], writes=["hh"], inc=s_ve)
            for dd in range(c.DC):
                fslot = dd % 2
                xsl = xf32[:, fslot * c.TT:(fslot + 1) * c.TT]

                def ldr(e, dd=dd, xsl=xsl, col0=col0):
                    src_ap = src.rearrange("(dc p) t -> p dc t", p=128)[:, dd:dd + 1, col0:col0 + c.TT]
                    return e.dma_start(xsl.rearrange("p (one t) -> p one t", one=1), src_ap)

                bld.emit("sync", ldr, reads=[src_key], writes=[f"xf{fslot}"], inc=sl_xf[fslot], inc_n=16)
                pd = PB[0 + fslot]
                for half in range(2):
                    ic0, ic1 = half * c.ICH, min(c.IC, (half + 1) * c.ICH)
                    if ic0 >= ic1:
                        continue
                    hslot = (dd * 2 + half) % 2
                    wd_sl = wds[:, hslot * c.ICH * 128: hslot * c.ICH * 128 + (ic1 - ic0) * 128]
                    bld.emit("gpsimd", lambda e, dd=dd, ic0=ic0, ic1=ic1, wd_sl=wd_sl: e.dma_start(
                        wd_sl, dwT[dd][:, ic0 * 128:ic1 * 128]),
                        writes=[f"wd{hslot}"], inc=sl_wd[hslot], inc_n=16)

                    def dmm(e, wd_sl=wd_sl, pd=pd, ic0=ic0, ic1=ic1):
                        for j, ic in enumerate(range(ic0, ic1)):
                            ins = e.matmul(pd[:, 0:c.TT], wd_sl[:, j * 128:(j + 1) * 128],
                                           hh[:, ic * c.TT:(ic + 1) * c.TT],
                                           start=(ic == 0), stop=(ic == c.IC - 1))
                        return ins

                    bld.emit("tensor", dmm, reads=[f"wd{hslot}", "hh"], writes=[f"PB{fslot}"], inc=s_mm)
                mo = xmo[:, fslot * c.TT:(fslot + 1) * c.TT]
                bld.emit("vector", lambda e, mo=mo, pd=pd, xsl=xsl: e.scalar_tensor_tensor(
                    mo, pd[:, 0:c.TT], cout_bc[:, 0:1], xsl, ALU.mult, ALU.add),
                    reads=[f"PB{fslot}", f"xf{fslot}", "cout_bc"], writes=[f"xmo{fslot}"], inc=s_ve)

                def stm(e, dd=dd, mo=mo, col0=col0):
                    dst_ap = dst.rearrange("(dc p) t -> p dc t", p=128)[:, dd:dd + 1, col0:col0 + c.TT]
                    return e.dma_start(dst_ap, mo.rearrange("p (one t) -> p one t", one=1))

                bld.emit("sync", stm, reads=[f"xmo{fslot}"], writes=[dst_key], inc=st_sems[fslot], inc_n=16)

    if upto == "ffn1":
        emit_ffn(x_in, "x_in", out_d, "out", sl_sto)
    else:
        emit_ffn(x_in, "x_in", xmid, "xmid", sl_stm)
    bld.barrier(sems)

    if upto == "ffn1":
        for s in sems:
            bld.raw_wait("sync", s, s.n)
        return _finish(nc, bld, psum_cms, sem_cms)
    bld.emit("sync", lambda e: e.dma_start(masks[:], masks_in), writes=["masks"], inc=s_const, inc_n=16)
    bld.emit("sync", lambda e: e.dma_start(ident[:], ident_in), writes=["ident"], inc=s_const, inc_n=16)
    for tt in range(c.NTT):
        emit_ln(xmid, "xmid", tt * c.TT, xln2, "proj_xln", proj=True)

    oc_order = (list(range(c.OCQ, c.OCQ + 2 * c.OCK)) + list(range(c.OCQ)))
    for oci, oc in enumerate(oc_order):
        slot = oci % 2
        w_sl = wa[:, slot * c.DC * 128:(slot + 1) * c.DC * 128]
        wsrc = wqT[oc] if oc < c.OCQ else (
            wkT[oc - c.OCQ] if oc < c.OCQ + c.OCK else wvT[oc - c.OCQ - c.OCK])
        bld.emit("gpsimd", lambda e, wsrc=wsrc, w_sl=w_sl: e.dma_start(w_sl, wsrc),
                 writes=[f"wa{slot}"], inc=sl_wa[slot], inc_n=16)
        for ph in range(c.PH):
            pb = PB[slot * 2 + (ph % 2)]
            pbkey = f"PB{slot * 2 + (ph % 2)}"

            def pmm(e, w_sl=w_sl, ph=ph, pb=pb):
                for dc in range(c.DC):
                    ins = e.matmul(pb[:, 0:c.PHW], w_sl[:, dc * 128:(dc + 1) * 128],
                                   xln2[:, dc * c.T + ph * c.PHW: dc * c.T + (ph + 1) * c.PHW],
                                   start=(dc == 0), stop=(dc == c.DC - 1))
                return ins

            bld.emit("tensor", pmm, reads=[f"wa{slot}", "proj_xln"], writes=[pbkey], inc=s_mm)
            if oc < c.OCQ:
                dstq = q_sb[:, oc * c.T + ph * c.PHW: oc * c.T + (ph + 1) * c.PHW]
                bld.emit("vector", lambda e, dstq=dstq, pb=pb: e.tensor_copy(dstq, pb[:, 0:c.PHW]),
                         reads=[pbkey], writes=[f"q_{oc}"], inc=s_ve)
            elif oc < c.OCQ + c.OCK:
                dstk = kst[:, ph * c.PHW:(ph + 1) * c.PHW]
                bld.emit("vector", lambda e, dstk=dstk, pb=pb: e.tensor_copy(dstk, pb[:, 0:c.PHW]),
                         reads=[pbkey], writes=["kst"], inc=s_ve)
            else:
                dstv = vst[:, ph * c.PHW:(ph + 1) * c.PHW]
                bld.emit("vector", lambda e, dstv=dstv, pb=pb: e.tensor_copy(dstv, pb[:, 0:c.PHW]),
                         reads=[pbkey], writes=["vst"], inc=s_ve)
        if c.OCQ <= oc < c.OCQ + c.OCK:
            kh = oc - c.OCQ
            bld.emit("sync", lambda e, kh=kh: e.dma_start(kv_contrib[kh * 128:(kh + 1) * 128, :], kst[:]),
                     reads=["kst"], writes=["kv_contrib"], inc=s_stkv, inc_n=16)
        elif oc >= c.OCQ + c.OCK:
            kh = oc - c.OCQ - c.OCK
            for tc in range(c.T // 128):
                psl = PBT[:, (tc % 4) * 256:(tc % 4) * 256 + 128]
                bld.emit("tensor", lambda e, tc=tc, psl=psl: e.transpose(
                    psl, vst[:, tc * 128:(tc + 1) * 128], ident[:]),
                    reads=["vst", "ident"], writes=["PBT"], inc=s_mm)
                bld.emit("vector", lambda e, tc=tc, psl=psl: e.tensor_copy(
                    vtmst[:, tc * 128:(tc + 1) * 128], psl),
                    reads=["PBT"], writes=["vtmst"], inc=s_ve)

            def stv(e, kh=kh):
                dst_ap = kv_contrib[c.KV:, :].rearrange("(tc t) kv -> t tc kv", t=128)[:, :, kh * 128:(kh + 1) * 128]
                src_ap = vtmst[:].rearrange("t (tc hd) -> t tc hd", hd=128)
                return e.dma_start(dst_ap, src_ap)

            bld.emit("sync", stv, reads=["vtmst"], writes=["kv_contrib"], inc=s_stkv, inc_n=16)
        if oci == 2 * c.OCK - 1:
            def ccop(e):
                return e.collective_compute(
                    "AllGather", ALU.bypass,
                    replica_groups=[[0, 4], [1, 5], [2, 6], [3, 7]],
                    ins=[kv_contrib_t.ap().opt()],
                    outs=[kv_gath_t.ap().opt()],
                )

            bld.emit("gpsimd", ccop, reads=["kv_contrib"], writes=["kv_gath"], inc=s_cc, inc_n=None)

    def setones(e):
        ap = v_aug[:].rearrange("t (cj kh x) -> t cj kh x", kh=c.NKV, x=129)[:, :, :, 128:129]
        return e.memset(ap, 1.0)

    bld.emit("vector", setones, reads=[], writes=["proj_xln", "v_aug"], inc=s_ve)

    CW = c.KV + c.T
    for r in range(2):
        for tc in range(c.NT):
            pos = (c.TPS - 1 - r) - 2 * tc

            def ldk(e, r=r, tc=tc, pos=pos):
                src_ap = kv_gath[r * CW: r * CW + c.KV, tc * 128:(tc + 1) * 128] \
                    .rearrange("(kh p) j -> p kh j", p=128)
                dst_ap = k_full[:].rearrange("p (kh s) -> p kh s", kh=c.NKV)[:, :, pos * 128:(pos + 1) * 128]
                return e.dma_start(dst_ap, src_ap)

            bld.emit("sync", ldk, reads=["kv_gath"], writes=["proj_xln", "k_full"], inc=s_kvl, inc_n=16)

            def ldv(e, r=r, tc=tc, pos=pos):
                src_ap = kv_gath[r * CW + c.KV + tc * 128: r * CW + c.KV + (tc + 1) * 128, :] \
                    .rearrange("t (kh hd) -> t kh hd", hd=128)
                dst_ap = v_aug[:, pos * VAW:(pos + 1) * VAW] \
                    .rearrange("t (kh x) -> t kh x", x=129)[:, :, 0:128]
                return e.dma_start(dst_ap, src_ap)

            bld.emit("sync", ldv, reads=["kv_gath"], writes=["proj_xln", "v_aug"], inc=s_kvl, inc_n=16)

    pair_idx = 0
    for qt in range(c.NT):
        span = c.SPAN[qt]
        ngr = (span + 3) // 4
        for kh in range(c.NKV):
            for pr in range(c.NREP // 2):
                heads = [kh * c.NREP + pr * 2, kh * c.NREP + pr * 2 + 1]
                avb = [2 + (pair_idx % 2) * 2, 3 + (pair_idx % 2) * 2]
                pair_idx += 1

                def emit_scores(g, hi, h):
                    w = min(4, span - g * 4)
                    ps = PB[hi]
                    ptk = f"pt{hi}{g % 2}"
                    pt_sl = ptb[:, (hi * 2 + g % 2) * 512:(hi * 2 + g % 2) * 512 + w * 128]

                    def smm(e, g=g, w=w, ps=ps, h=h, kh=kh, qt=qt):
                        for ci in range(w):
                            cj = g * 4 + ci
                            ins = e.matmul(ps[:, ci * 128:(ci + 1) * 128],
                                           k_full[:, kh * c.S + cj * 128: kh * c.S + (cj + 1) * 128],
                                           q_sb[:, h * c.T + qt * 128: h * c.T + (qt + 1) * 128],
                                           start=True, stop=True)
                        return ins

                    bld.emit("tensor", smm, reads=["k_full", f"q_{h}"], writes=[f"PB{hi}"], inc=s_mm)
                    bld.emit("scalar", lambda e, pt_sl=pt_sl, ps=ps, w=w: e.activation(
                        pt_sl, ps[:, 0:w * 128], AF.Exp, scale=float(c.SCALE)),
                        reads=[f"PB{hi}"], writes=[ptk], inc=s_se)
                    if g == ngr - 1:
                        mk = masks[:, qt * 256:(qt + 1) * 256]
                        msl = ptb[:, (hi * 2 + g % 2) * 512 + (w - 2) * 128:(hi * 2 + g % 2) * 512 + w * 128]
                        bld.emit("vector", lambda e, msl=msl, mk=mk: e.tensor_tensor(msl, msl, mk, ALU.mult),
                                 reads=[ptk, "masks"], writes=[ptk], inc=s_ve)

                def emit_av(g, hi):
                    w = min(4, span - g * 4)
                    ptk = f"pt{hi}{g % 2}"

                    def avmm(e, g=g, w=w, hi=hi, kh=kh, avb=tuple(avb), ngr=ngr):
                        for ci in range(w):
                            cj = g * 4 + ci
                            ins = e.matmul(PB[avb[hi]][:, 0:129],
                                           ptb[:, (hi * 2 + g % 2) * 512 + ci * 128:(hi * 2 + g % 2) * 512 + (ci + 1) * 128],
                                           v_aug[:, cj * VAW + kh * 129: cj * VAW + (kh + 1) * 129],
                                           start=(g == 0 and ci == 0),
                                           stop=(g == ngr - 1 and ci == w - 1))
                        return ins

                    bld.emit("tensor", avmm, reads=["v_aug", ptk], writes=[f"PB{avb[hi]}"], inc=s_mm)

                for g in range(ngr + 1):
                    if g < ngr:
                        for hi, h in enumerate(heads):
                            emit_scores(g, hi, h)
                    if g >= 1:
                        for hi in range(2):
                            emit_av(g - 1, hi)

                for hi, h in enumerate(heads):
                    pav = PB[avb[hi]]
                    pavk = f"PB{avb[hi]}"
                    rc = rec_sb[:, hi:hi + 1]
                    tsl = tmp_tm[:, hi * 128:(hi + 1) * 128]
                    bld.emit("vector", lambda e, rc=rc, pav=pav: e.reciprocal(rc, pav[:, 128:129]),
                             reads=[pavk], writes=[f"rc{hi}"], inc=s_ve)
                    bld.emit("vector", lambda e, tsl=tsl, pav=pav, rc=rc: e.tensor_scalar(
                        tsl, pav[:, 0:128], rc[:, 0:1], None, ALU.mult),
                        reads=[pavk, f"rc{hi}"], writes=[f"tsl{hi}"], inc=s_ve)
                    psl = PBT[:, (hi * 2) * 256:(hi * 2) * 256 + 128]
                    bld.emit("tensor", lambda e, psl=psl, tsl=tsl: e.transpose(psl, tsl, ident[:]),
                             reads=[f"tsl{hi}", "ident"], writes=["PBT"], inc=s_mm)
                    dst_ao = q_sb[:, h * c.T + qt * 128: h * c.T + (qt + 1) * 128]
                    bld.emit("vector", lambda e, dst_ao=dst_ao, psl=psl: e.tensor_copy(dst_ao, psl),
                             reads=["PBT"], writes=[f"q_{h}"], inc=s_ve)

    for dd in range(c.DC):
        slot = dd % 2
        w_sl = wa[:, slot * c.DC * 128:(slot + 1) * c.DC * 128]
        bld.emit("gpsimd", lambda e, dd=dd, w_sl=w_sl: e.dma_start(w_sl, woT[dd]),
                 writes=[f"wa{slot}"], inc=sl_wa[slot], inc_n=16)
        for ph in range(c.PH):
            idx = dd * c.PH + ph
            fslot = idx % 2
            pb = PB[slot * 2 + (ph % 2)]
            pbkey = f"PB{slot * 2 + (ph % 2)}"

            def womm(e, w_sl=w_sl, ph=ph, pb=pb):
                for hc in range(c.HC):
                    ins = e.matmul(pb[:, 0:c.PHW], w_sl[:, hc * 128:(hc + 1) * 128],
                                   q_sb[:, hc * c.T + ph * c.PHW: hc * c.T + (ph + 1) * c.PHW],
                                   start=(hc == 0), stop=(hc == c.HC - 1))
                return ins

            bld.emit("tensor", womm, reads=[f"wa{slot}"] + [f"q_{hq}" for hq in range(c.NH)],
                     writes=[pbkey], inc=s_mm)
            xsl = xf32[:, fslot * c.TT: fslot * c.TT + c.PHW]

            def ldxm(e, dd=dd, ph=ph, xsl=xsl):
                return e.dma_start(xsl.rearrange("p (one t) -> p one t", one=1),
                                   xmid.rearrange("(dc p) t -> p dc t", p=128)
                                   [:, dd:dd + 1, ph * c.PHW:(ph + 1) * c.PHW])

            bld.emit("sync", ldxm, reads=["xmid"], writes=[f"xf{fslot}"], inc=sl_xf[fslot], inc_n=16)
            mo = xmo[:, fslot * c.TT: fslot * c.TT + c.PHW]
            bld.emit("vector", lambda e, mo=mo, pb=pb, xsl=xsl: e.tensor_tensor(
                mo, pb[:, 0:c.PHW], xsl, ALU.add),
                reads=[pbkey, f"xf{fslot}"], writes=[f"xmo{fslot}"], inc=s_ve)

            def stxa(e, dd=dd, ph=ph, mo=mo):
                return e.dma_start(xattn.rearrange("(dc p) t -> p dc t", p=128)
                                   [:, dd:dd + 1, ph * c.PHW:(ph + 1) * c.PHW],
                                   mo.rearrange("p (one t) -> p one t", one=1))

            bld.emit("sync", stxa, reads=[f"xmo{fslot}"], writes=["xattn"], inc=sl_sta[fslot], inc_n=16)

    bld.barrier(sems)

    if upto == "attn":
        for dd in range(c.DC):
            for tt2 in range(c.NTT):
                fslot = (dd * c.NTT + tt2) % 2
                xsl = xf32[:, fslot * c.TT:(fslot + 1) * c.TT]
                bld.emit("sync", lambda e, dd=dd, tt2=tt2, xsl=xsl: e.dma_start(
                    xsl.rearrange("p (one t) -> p one t", one=1),
                    xattn.rearrange("(dc p) t -> p dc t", p=128)[:, dd:dd + 1, tt2 * c.TT:(tt2 + 1) * c.TT]),
                    reads=["xattn"], writes=[f"xf{fslot}"], inc=sl_xf[fslot], inc_n=16)
                bld.emit("sync", lambda e, dd=dd, tt2=tt2, xsl=xsl: e.dma_start(
                    out_d.rearrange("(dc p) t -> p dc t", p=128)[:, dd:dd + 1, tt2 * c.TT:(tt2 + 1) * c.TT],
                    xsl.rearrange("p (one t) -> p one t", one=1)),
                    reads=[f"xf{fslot}"], writes=["out"], inc=sl_sto[fslot], inc_n=16)
    else:
        emit_ffn(xattn, "xattn", out_d, "out", sl_sto)

    for s in sems:
        bld.raw_wait("sync", s, s.n)
    return _finish(nc, bld, psum_cms, sem_cms)


def _finish(nc, bld, psum_cms, sem_cms):
    with nc.Block() as block:
        @block.tensor
        def _(e):
            for op in bld.plan["tensor"]:
                op(e)

        @block.vector
        def _(e):
            for op in bld.plan["vector"]:
                op(e)

        @block.scalar
        def _(e):
            for op in bld.plan["scalar"]:
                op(e)

        @block.gpsimd
        def _(e):
            for op in bld.plan["gpsimd"]:
                op(e)

        @block.sync
        def _(e):
            for op in bld.plan["sync"]:
                op(e)

    for cm in psum_cms[::-1]:
        cm.__exit__(None, None, None)
    for cm in sem_cms[::-1]:
        cm.__exit__(None, None, None)
    return nc



def _tile_lhsT(w, out_chunks, in_chunks):
    return np.ascontiguousarray(
        np.asarray(w).reshape(out_chunks, 128, in_chunks, 128).transpose(0, 3, 2, 1)
    ).reshape(out_chunks, 128, in_chunks * 128)


def _tile_nat(w, in_chunks, out_chunks):
    return np.ascontiguousarray(
        np.asarray(w).reshape(in_chunks, 128, out_chunks, 128).transpose(2, 1, 0, 3)
    ).reshape(out_chunks, 128, in_chunks * 128)


def make_inputs(c, x, gate_w, up_w, down_w, wq, wk, wv, wo, gate_s, up_s, down_s):
    gwT = _tile_lhsT(gate_w, c.IC, c.DC)
    uwT = _tile_lhsT(up_w, c.IC, c.DC)
    dwT = _tile_lhsT(down_w, c.DC, c.IC)
    wqT = _tile_nat(np.asarray(wq, np.float32), c.DC, c.OCQ)
    wkT = _tile_nat(np.asarray(wk, np.float32), c.DC, c.OCK)
    wvT = _tile_nat(np.asarray(wv, np.float32), c.DC, c.OCK)
    woT = _tile_nat(np.asarray(wo, np.float32), c.HC, c.DC)
    scal = np.array([[np.asarray(gate_s, np.float32).reshape(-1)[0],
                      np.asarray(up_s, np.float32).reshape(-1)[0],
                      np.asarray(down_s, np.float32).reshape(-1)[0]]], dtype=np.float32)
    ident = np.eye(128, dtype=ml_dtypes.bfloat16)
    tri = np.tril(np.ones((128, 128), np.float32)).T.astype(ml_dtypes.bfloat16)
    onesm = np.ones((128, 128), ml_dtypes.bfloat16)
    zerosm = np.zeros((128, 128), ml_dtypes.bfloat16)

    xf = np.asarray(x, np.float32)
    in_maps = []
    for core in range(8):
        b = core % 4
        p = core // 4
        pos_list = [c.TPS - 1 - p - 2 * k for k in range(c.NT)]
        xc = np.ascontiguousarray(
            xf[b].reshape(c.TPS, 128, c.D)[pos_list].reshape(c.T, c.D).T)
        mk = np.empty((c.NT, 2, 128, 128), ml_dtypes.bfloat16)
        for k in range(c.NT):
            mk[k][0] = onesm if p == 0 else tri
            mk[k][1] = tri if p == 0 else zerosm
        masks = np.ascontiguousarray(mk.transpose(2, 0, 1, 3)).reshape(128, c.NT * 2 * 128)
        in_maps.append({
            "x": xc, "gwT": gwT, "uwT": uwT, "dwT": dwT,
            "wqT": wqT, "wkT": wkT, "wvT": wvT, "woT": woT,
            "scal": scal, "masks": masks, "ident": ident,
        })
    return in_maps


def assemble_output(c, results):
    y = np.empty((c.B, c.S, c.D), np.float32)
    for core in range(8):
        b = core % 4
        p = core // 4
        pos_list = [c.TPS - 1 - p - 2 * k for k in range(c.NT)]
        oc = results[core]["out"]
        y[b].reshape(c.TPS, 128, c.D)[pos_list] = oc.T.reshape(c.NT, 128, c.D)
    return y


def run(cfg, x, gate_w, gate_s, up_w, up_s, down_w, down_s, wq, wk, wv, wo, upto="full"):
    in_maps = make_inputs(cfg, x, gate_w, up_w, down_w, wq, wk, wv, wo,
                          gate_s, up_s, down_s)
    nc = build_nc(cfg, upto=upto)
    res = run_bass_kernel_spmd(nc, in_maps, core_ids=list(range(8)))
    return assemble_output(cfg, res.results)


def kernel(L0_attn_x, L0_attn_wq, L0_attn_wk, L0_attn_wv, L0_attn_wo,
           L0_ffn_gate_w, L0_ffn_gate_s, L0_ffn_up_w, L0_ffn_up_s,
           L0_ffn_down_w, L0_ffn_down_s):
    return run(CFG_FULL, L0_attn_x,
               L0_ffn_gate_w, L0_ffn_gate_s, L0_ffn_up_w, L0_ffn_up_s,
               L0_ffn_down_w, L0_ffn_down_s,
               L0_attn_wq, L0_attn_wk, L0_attn_wv, L0_attn_wo)
